# version 02
# baseline (speedup 1.0000x reference)
import numpy as np
import ml_dtypes
import concourse.bass as bass
import concourse.mybir as mybir
from contextlib import ExitStack
from concourse.bass_utils import run_bass_kernel_spmd

F32 = mybir.dt.float32
BF16 = mybir.dt.bfloat16
FP16 = mybir.dt.float16
AluOp = mybir.AluOpType
AF = mybir.ActivationFunctionType
AX = mybir.AxisListType.X

NCORES = 8
B, S, E = 16, 4096, 1024
H, D, WIN = 16, 64, 512
FFI = 2816
T = (B // NCORES) * S
NB = T // WIN
EC = E // 128
IC = FFI // 128
KAUG = 69
CSTAB = 20.0
MAGIC = float(np.float32(1.5 * 2 ** 23))
QMAX = 127.0

QLO = [0, 0, 0, 0, 0, 128, 256, 384]
QHI = [128, 256, 384, 512, 512, 512, 512, 512]
WW = [QHI[c] - QLO[c] for c in range(8)]
MOFF = [0]
for c in range(8):
    MOFF.append(MOFF[-1] + WW[c])

_SLOPES = np.exp2(-(np.arange(H, dtype=np.float64) + 1.0) * 8.0 / H)


def _bf(x):
    return np.asarray(x, dtype=np.float64).astype(ml_dtypes.bfloat16)


def _hi_lo(v):
    hi = _bf(v)
    lo = _bf(np.asarray(v, dtype=np.float64) - hi.astype(np.float64))
    return hi, lo


def _make_consts():
    j = np.arange(WIN, dtype=np.float64)
    v = -(_SLOPES[:, None]) * (512.0 + j[None, :])
    khi, klo = _hi_lo(v.reshape(-1))
    ones = np.ones(H * WIN)
    zer = np.zeros(H * WIN)
    kaug_cur = _bf(np.stack([khi.astype(np.float64), klo.astype(np.float64), ones, zer, zer]))
    kaug_prev = _bf(np.stack([khi.astype(np.float64), klo.astype(np.float64), ones, ones, ones]))
    brow = (_SLOPES[:, None] * j[None, :] - CSTAB).reshape(-1)
    brow_first = np.repeat(512.0 * _SLOPES - CSTAB, WIN)
    h512, l512 = _hi_lo(512.0 * _SLOPES)
    qaug = _bf(np.stack([
        np.ones(H * WIN), np.ones(H * WIN), brow,
        np.repeat(h512.astype(np.float64), WIN),
        np.repeat(l512.astype(np.float64), WIN),
    ]))
    qaug_first = _bf(np.stack([
        np.ones(H * WIN), np.ones(H * WIN), brow_first,
        np.repeat(h512.astype(np.float64), WIN),
        np.repeat(l512.astype(np.float64), WIN),
    ]))
    masks = np.zeros((128, MOFF[-1]), dtype=np.float64)
    for c in range(8):
        rr = np.arange(128)[:, None]
        q = (QLO[c] + np.arange(WW[c]))[None, :]
        kj = c * 128 + rr
        masks[:, MOFF[c]:MOFF[c + 1]] = ((kj - 512 <= q) & (q <= kj))
    return dict(kaug_cur=np.asarray(kaug_cur), kaug_prev=np.asarray(kaug_prev),
                qaug=np.asarray(qaug), qaug_first=np.asarray(qaug_first),
                masks=np.asarray(_bf(masks)),
                ident_bf=np.eye(128, dtype=ml_dtypes.bfloat16),
                ident_f32=np.eye(128, dtype=np.float32),
                ones_row=np.ones((1, 128), dtype=np.float32),
                ones_col_bf=np.ones((128, 1), dtype=ml_dtypes.bfloat16))


class Ctr:
    def __init__(self, ctx, nc, name):
        self.h = ctx.enter_context(nc.semaphore(name))
        self.n = 0


class DmaQ:
    NROT = 8

    def __init__(self, ctx, nc, name, eng):
        self.sems = [Ctr(ctx, nc, f"{name}{i}") for i in range(self.NROT)]
        self.eng = eng
        self.i = 0

    def dma(self, r, fn):
        s = self.sems[self.i % self.NROT]
        self.i += 1
        r.op(self.eng, fn, s, 16)
        return (s, s.n)

    def wait(self, r, eng, tok):
        if tok is not None:
            s, v = tok
            r.wait(eng, s, v)

    def wait_all(self, r, eng):
        for s in self.sems:
            r.wait(eng, s, s.n)


class Rec:
    ENGS = ("sync", "gpsimd", "tensor", "scalar", "vector")

    def __init__(self):
        self.ops = {k: [] for k in self.ENGS}

    def op(self, eng, fn, inc=None, k=1):
        if inc is None:
            self.ops[eng].append(fn)
        else:
            inc.n += k
            self.ops[eng].append(lambda e, f=fn, h=inc.h, kk=k: f(e).then_inc(h, kk))
        return inc.n if inc is not None else None

    def wait(self, eng, ctr, val=None):
        v = ctr.n if val is None else val
        if v and v > 0:
            self.ops[eng].append(lambda e, h=ctr.h, vv=v: e.wait_ge(h, vv))

    def play(self, block):
        for eng in self.ENGS:
            ops = self.ops[eng]

            def run(e, ops=ops):
                for fn in ops:
                    fn(e)
            getattr(block, eng)(run)



def build_l1():
    nc = bass.Bass("TRN2", target_bir_lowering=False)
    C = _make_consts()
    x_d = nc.dram_tensor("x", [T, E], F32, kind="ExternalInput").ap()
    win_d = nc.dram_tensor("w_in", [3 * E, E], F32, kind="ExternalInput").ap()
    wout_d = nc.dram_tensor("w_out", [E, E], F32, kind="ExternalInput").ap()
    x1_d = nc.dram_tensor("x1", [E, T], F32, kind="ExternalOutput").ap()
    lmax_d = nc.dram_tensor("lmax", [128, 1], F32, kind="ExternalOutput").ap()

    kaugc_d = nc.inline_tensor(C["kaug_cur"], "c_kaugc").ap()
    kaugp_d = nc.inline_tensor(C["kaug_prev"], "c_kaugp").ap()
    qaug_d = nc.inline_tensor(C["qaug"], "c_qaug").ap()
    qaugf_d = nc.inline_tensor(C["qaug_first"], "c_qaugf").ap()
    mask_d = nc.inline_tensor(C["masks"], "c_masks").ap()
    idb_d = nc.inline_tensor(C["ident_bf"], "c_idb").ap()
    idf_d = nc.inline_tensor(C["ident_f32"], "c_idf").ap()
    ones_d = nc.inline_tensor(C["ones_row"], "c_ones").ap()
    onesc_d = nc.inline_tensor(C["ones_col_bf"], "c_onescol").ap()

    with ExitStack() as ctx:
        e_ = ctx.enter_context
        winT = e_(nc.sbuf_tensor("winT", [128, 24 * 8 * 128], BF16))
        woutT = e_(nc.sbuf_tensor("woutT", [128, 8 * 8 * 128], BF16))
        mask_s = e_(nc.sbuf_tensor("mask_s", [128, MOFF[-1]], BF16))
        idb_s = e_(nc.sbuf_tensor("idb_s", [128, 128], BF16))
        idf_s = e_(nc.sbuf_tensor("idf_s", [128, 128], F32))
        ones_s = e_(nc.sbuf_tensor("ones_s", [1, 128], F32))
        onesc_s = e_(nc.sbuf_tensor("onesc_s", [128, 1], BF16))
        qp = e_(nc.sbuf_tensor("qp", [KAUG, H * WIN], BF16))
        kb = [e_(nc.sbuf_tensor(f"kb{i}", [KAUG, H * WIN], BF16)) for i in range(2)]
        vb = [e_(nc.sbuf_tensor(f"vb{i}", [128, 4 * H * 65], BF16)) for i in range(2)]
        x_tok = e_(nc.sbuf_tensor("x_tok", [128, 4 * E], BF16))
        xT = e_(nc.sbuf_tensor("xT", [128, EC * WIN], BF16))
        oT = e_(nc.sbuf_tensor("oT", [128, EC * WIN], BF16))
        resid = e_(nc.sbuf_tensor("resid", [128, EC * WIN], BF16))
        sq = e_(nc.sbuf_tensor("sq", [128, 2 * WIN], BF16))
        x1t = e_(nc.sbuf_tensor("x1t", [128, EC * WIN], F32))
        pT = [e_(nc.sbuf_tensor(f"pT{i}", [128, WIN], BF16)) for i in range(2)]
        o_un = [e_(nc.sbuf_tensor(f"o_un{i}", [64, WIN], BF16)) for i in range(2)]
        rden = [e_(nc.sbuf_tensor(f"rden{i}", [1, WIN], F32)) for i in range(2)]
        wstg = [resid[:, 0:E], oT[:, 0:E]]
        mean_s = e_(nc.sbuf_tensor("mean_s", [1, WIN], F32))
        ex2_s = e_(nc.sbuf_tensor("ex2_s", [1, WIN], F32))
        m2_s = e_(nc.sbuf_tensor("m2_s", [1, WIN], F32))
        var_s = e_(nc.sbuf_tensor("var_s", [1, WIN], F32))
        lnv_s = e_(nc.sbuf_tensor("lnv_s", [1, WIN], F32))
        rstd_s = e_(nc.sbuf_tensor("rstd_s", [1, WIN], F32))
        tmpc = e_(nc.sbuf_tensor("tmpc", [128, WIN], BF16))
        amax = e_(nc.sbuf_tensor("amax", [128, 1], F32))
        tmax = e_(nc.sbuf_tensor("tmax", [128, NB * EC], F32))
        gm_s = e_(nc.sbuf_tensor("gm_s", [1, 1], F32))
        eps_s = e_(nc.sbuf_tensor("eps_s", [1, 1], F32))
        tr_a = e_(nc.psum_tensor("tr_a", [128, 1024], BF16))
        tr_b = e_(nc.psum_tensor("tr_b", [128, 1024], BF16))
        mm_a = e_(nc.psum_tensor("mm_a", [128, 512], F32))
        mm_b = e_(nc.psum_tensor("mm_b", [128, 512], F32))
        sc_a = e_(nc.psum_tensor("sc_a", [128, 512], F32))
        sc_b = e_(nc.psum_tensor("sc_b", [128, 512], F32))
        ot_a = e_(nc.psum_tensor("ot_a", [128, 512], F32))
        ot_b = e_(nc.psum_tensor("ot_b", [128, 512], F32))
        sc_pp, ot_pp, mm_pp, tr_pp = [sc_a, sc_b], [ot_a, ot_b], [mm_a, mm_b], [tr_a, tr_b]

        gq = DmaQ(ctx, nc, "gq", "gpsimd")
        yq = DmaQ(ctx, nc, "yq", "sync")
        sT = Ctr(ctx, nc, "sT")
        sS = Ctr(ctx, nc, "sS")
        sV = Ctr(ctx, nc, "sV")
        r = Rec()

        for dst, src in ((mask_s[:], mask_d), (idb_s[:], idb_d), (idf_s[:], idf_d),
                         (ones_s[:], ones_d), (onesc_s[:], onesc_d),
                         ):
            gq.dma(r, lambda g, d=dst, s=src: g.dma_start(out=d, in_=s))
        gq.wait_all(r, "vector")
        gq.wait_all(r, "tensor")
        r.op("vector", lambda v: v.memset(eps_s[:], 1e-5), sV)
        for i in range(2):
            vap = vb[i][:].rearrange("p (g c) -> p g c", c=65)[:, :, 64:65]
            r.op("vector", lambda v, a=vap: v.memset(a, 1.0), sV)

        last_stg_T = {}
        last_tr_S = {}
        gidx = 0
        for name, wd, nch in (("win", win_d, 24), ("wout", wout_d, 8)):
            for fc in range(nch):
                par = gidx % 2
                stg = wstg[par]
                if par in last_stg_T:
                    r.wait("gpsimd", sT, last_stg_T[par])
                tok_w = gq.dma(r, lambda g, d=stg, s=wd[fc * 128:(fc + 1) * 128, :]:
                               g.dma_start(out=d, in_=s))
                bank = tr_pp[par]
                gq.wait(r, "tensor", tok_w)
                if par in last_tr_S:
                    r.wait("tensor", sS, last_tr_S[par])
                for ecc in range(8):
                    lst = (ecc == 7)
                    fn = (lambda t, o=bank[:, ecc * 128:(ecc + 1) * 128],
                          i_=stg[:, ecc * 128:(ecc + 1) * 128], idn=idb_s[:]:
                          t.transpose(o, i_, idn))
                    r.op("tensor", fn, sT if lst else None)
                last_stg_T[par] = sT.n
                r.wait("scalar", sT, sT.n)
                if name == "win":
                    dst = winT[:].rearrange("p (ec q) -> p ec q", q=24 * 128)[
                        :, :, fc * 128:(fc + 1) * 128]
                    src = bank[:].rearrange("p (ec q) -> p ec q", q=128)
                else:
                    dst = woutT[:, fc * 8 * 128:(fc + 1) * 8 * 128]
                    src = bank[:]
                r.op("scalar", lambda s, d=dst, so=src: s.copy(d, so), sS)
                last_tr_S[par] = sS.n
                gidx += 1

        xtok_free_T = 0
        xT_evac_S = 0
        attn_done_T = 0
        prev_res_V = 0
        prev_x1_V = 0
        spill_sY = {}
        for b in range(NB):
            first = (b % 8 == 0)
            cur, prev = b % 2, 1 - (b % 2)

            if b > 0:
                r.wait("gpsimd", sT, xtok_free_T)
            src = x_d[b * WIN:(b + 1) * WIN, :].rearrange("(c p) e -> p c e", p=128)
            dst = x_tok[:].rearrange("p (c e) -> p c e", e=E)
            tok_x = gq.dma(r, lambda g, d=dst, s=src: g.dma_start(out=d, in_=s))

            gq.wait(r, "tensor", tok_x)
            if b > 0:
                r.wait("scalar", sV, prev_res_V)
            for g in range(4):
                par = g % 2
                bank = tr_pp[par]
                r.wait("tensor", sS, last_tr_S[par])
                for k in range(8):
                    ecc, tc = 2 * g + k // 4, k % 4
                    lst = (k == 7)
                    fn = (lambda t, o=bank[:, k * 128:(k + 1) * 128],
                          i_=x_tok[:, tc * E + ecc * 128: tc * E + ecc * 128 + 128],
                          idn=idb_s[:]: t.transpose(o, i_, idn))
                    r.op("tensor", fn, sT if lst else None)
                r.wait("scalar", sT, sT.n)
                r.op("scalar", lambda s, d=xT[:, g * 1024:(g + 1) * 1024],
                     so=bank[:]: s.copy(d, so), sS)
                last_tr_S[par] = sS.n
            xtok_free_T = sT.n
            xT_evac_S = sS.n

            if b > 0:
                r.wait("gpsimd", sT, attn_done_T)
            tok_ka = gq.dma(r, lambda g, d=kb[cur][64:69, :], s=kaugc_d:
                            g.dma_start(out=d, in_=s))
            tok_qa = gq.dma(r, lambda g, d=qp[64:69, :], s=(qaugf_d if first else qaug_d):
                            g.dma_start(out=d, in_=s))
            tok_kp = None
            if not first:
                tok_kp = gq.dma(r, lambda g, d=kb[prev][64:69, :], s=kaugp_d:
                                g.dma_start(out=d, in_=s))

            r.wait("tensor", sS, xT_evac_S)
            if b > 0:
                r.wait("tensor", sV, prev_x1_V)
            evac_S = {}
            for fc in range(16):
                bank = mm_pp[fc % 2]
                if fc >= 2:
                    r.wait("tensor", sS, evac_S[fc - 2])
                for ecc in range(EC):
                    fn = (lambda t, o=bank[:],
                          l=winT[:, (ecc * 24 + fc) * 128:(ecc * 24 + fc) * 128 + 128],
                          rr=xT[:, ecc * 512:(ecc + 1) * 512],
                          st=(ecc == 0), sp=(ecc == EC - 1):
                          t.matmul(o, l, rr, start=st, stop=sp, skip_group_check=True))
                    r.op("tensor", fn, sT if ecc == EC - 1 else None)
                r.wait("scalar", sT, sT.n)
                if fc < 8:
                    h0 = 2 * fc
                    r.op("scalar", lambda s, d=qp[0:64, h0 * 512:(h0 + 1) * 512],
                         so=bank[0:64, :]: s.activation(d, so, AF.Copy, scale=0.125))
                    r.op("scalar", lambda s, d=qp[0:64, (h0 + 1) * 512:(h0 + 2) * 512],
                         so=bank[64:128, :]: s.activation(d, so, AF.Copy, scale=0.125), sS)
                else:
                    h0 = 2 * (fc - 8)
                    r.op("scalar", lambda s, d=kb[cur][0:64, h0 * 512:(h0 + 1) * 512],
                         so=bank[0:64, :]: s.copy(d, so))
                    r.op("scalar", lambda s, d=kb[cur][0:64, (h0 + 1) * 512:(h0 + 2) * 512],
                         so=bank[64:128, :]: s.copy(d, so), sS)
                evac_S[fc] = sS.n
            for j in range(8):
                tc, hf = j // 2, j % 2
                bank = mm_pp[j % 2]
                r.wait("tensor", sS, evac_S[14 + j] if j < 2 else evac_S[16 + j - 2])
                for ecc in range(EC):
                    fn = (lambda t, o=bank[:],
                          l=xT[:, ecc * 512 + tc * 128: ecc * 512 + tc * 128 + 128],
                          rr=winT[:, (ecc * 24 + 16 + hf * 4) * 128:(ecc * 24 + 16 + hf * 4 + 4) * 128],
                          st=(ecc == 0), sp=(ecc == EC - 1):
                          t.matmul(o, l, rr, start=st, stop=sp, skip_group_check=True))
                    r.op("tensor", fn, sT if ecc == EC - 1 else None)
                r.wait("scalar", sT, sT.n)
                dstv = vb[cur][:, tc * 1040 + hf * 8 * 65: tc * 1040 + (hf * 8 + 8) * 65] \
                    .rearrange("p (h c) -> p h c", c=65)[:, :, 0:64]
                srcv = bank[:].rearrange("p (h c) -> p h c", c=64)
                r.op("scalar", lambda s, d=dstv, so=srcv: s.copy(d, so), sS)
                evac_S[16 + j] = sS.n
            qkv_done_S = sS.n

            cs = [4, 5, 6, 7] if first else [3, 4, 0, 1, 2, 5, 6, 7]
            ncs = len(cs)
            r.wait("tensor", sS, qkv_done_S)
            gq.wait(r, "tensor", tok_ka)
            gq.wait(r, "tensor", tok_qa)
            gq.wait(r, "tensor", tok_kp)
            exp_S, pv_T, recip_V, oun_S, bcast_T, ott_V = {}, {}, {}, {}, {}, {}
            pend = None
            for h in range(H):
                for i, c in enumerate(cs):
                    kbu = kb[cur] if c >= 4 else kb[prev]
                    vbu = vb[cur] if c >= 4 else vb[prev]
                    w = WW[c]
                    scb = sc_pp[i % 2]
                    if (h, i - 2) in exp_S:
                        r.wait("tensor", sS, exp_S[(h, i - 2)])
                    elif h > 0 and i < 2:
                        r.wait("tensor", sS, exp_S[(h - 1, ncs - 2 + i)])
                    cb = h * 512 + (c % 4) * 128
                    fn = (lambda t, o=scb[0:128, 0:w], l=kbu[0:KAUG, cb:cb + 128],
                          rr=qp[0:KAUG, h * 512 + QLO[c]: h * 512 + QHI[c]]:
                          t.matmul(o, l, rr, start=True, stop=True, skip_group_check=True))
                    sc_T = r.op("tensor", fn, sT)
                    if i == 1 and pend is not None:
                        pend()
                        pend = None
                    r.wait("scalar", sT, sc_T)
                    if (h, i - 2) in pv_T:
                        r.wait("scalar", sT, pv_T[(h, i - 2)])
                    elif h > 0 and i < 2:
                        r.wait("scalar", sT, pv_T[(h - 1, ncs - 2 + i)])
                    exp_S[(h, i)] = r.op(
                        "scalar", lambda s, d=pT[i % 2][0:128, 0:w], so=scb[0:128, 0:w]:
                        s.activation(d, so, AF.Exp), sS)
                    r.wait("vector", sS, exp_S[(h, i)])
                    mv = r.op("vector", lambda v, a=pT[i % 2][0:128, 0:w],
                              m=mask_s[:, MOFF[c]:MOFF[c] + w]:
                              v.tensor_tensor(a, a, m, AluOp.mult), sV)
                    r.wait("tensor", sV, mv)
                    if i == 0:
                        if (h - 2) in recip_V:
                            r.wait("tensor", sV, recip_V[h - 2])
                        if (h - 2) in oun_S:
                            r.wait("tensor", sS, oun_S[h - 2])
                    fn = (lambda t, o=ot_pp[h % 2][0:65, QLO[c]:QHI[c]],
                          l=vbu[:, (c % 4) * 1040 + h * 65: (c % 4) * 1040 + h * 65 + 65],
                          rr=pT[i % 2][0:128, 0:w], st=(i == 0), sp=(i == ncs - 1):
                          t.matmul(o, l, rr, start=st, stop=sp, skip_group_check=True))
                    pv_T[(h, i)] = r.op("tensor", fn, sT)
                r.wait("vector", sT, pv_T[(h, ncs - 1)])
                if (h - 2) in bcast_T:
                    r.wait("vector", sT, bcast_T[h - 2])
                recip_V[h] = r.op("vector", lambda v, d=rden[h % 2][:],
                                  so=ot_pp[h % 2][64:65, 0:512]: v.reciprocal(d, so), sV)
                r.wait("scalar", sV, recip_V[h])
                if (h - 2) in ott_V:
                    r.wait("scalar", sV, ott_V[h - 2])
                oun_S[h] = r.op("scalar", lambda s, d=o_un[h % 2][:],
                                so=ot_pp[h % 2][0:64, 0:512]: s.copy(d, so), sS)

                def mk(h):
                    def doit():
                        r.wait("tensor", sV, recip_V[h])
                        if (h - 2) in ott_V:
                            r.wait("tensor", sV, ott_V[h - 2])
                        fn = (lambda t, o=mm_pp[h % 2][0:64, :], l=ones_s[0:1, 0:64],
                              rr=rden[h % 2][:]:
                              t.matmul(o, l, rr, start=True, stop=True, skip_group_check=True))
                        bcast_T[h] = r.op("tensor", fn, sT)
                        r.wait("vector", sT, bcast_T[h])
                        r.wait("vector", sS, oun_S[h])
                        dsto = oT[(h % 2) * 64:(h % 2) * 64 + 64,
                                  (h // 2) * 512:(h // 2 + 1) * 512]
                        ott_V[h] = r.op("vector", lambda v, d=dsto, a=o_un[h % 2][:],
                                        bb=mm_pp[h % 2][0:64, :]:
                                        v.tensor_tensor(d, a, bb, AluOp.mult), sV)
                    return doit
                pend = mk(h)
            pend()
            attn_oT_V = sV.n
            attn_done_T = sT.n

            res_V, sq_S, sqmm_T = {}, {}, {}
            for fc in range(EC):
                bank = sc_pp[fc % 2]
                if fc < 2:
                    r.wait("tensor", sS, exp_S[(H - 1, ncs - 2 + fc)])
                    r.wait("tensor", sV, attn_oT_V)
                else:
                    r.wait("tensor", sV, res_V[fc - 2])
                for ecc in range(EC):
                    fn = (lambda t, o=bank[:],
                          l=woutT[:, (fc * 8 + ecc) * 128:(fc * 8 + ecc) * 128 + 128],
                          rr=oT[:, ecc * 512:(ecc + 1) * 512],
                          st=(ecc == 0), sp=(ecc == EC - 1):
                          t.matmul(o, l, rr, start=st, stop=sp, skip_group_check=True))
                    r.op("tensor", fn, sT if ecc == EC - 1 else None)
                r.wait("vector", sT, sT.n)
                res_V[fc] = r.op("vector", lambda v, d=resid[:, fc * 512:(fc + 1) * 512],
                                 a=bank[:], x_=xT[:, fc * 512:(fc + 1) * 512]:
                                 v.scalar_tensor_tensor(d, a, 1.0, x_, AluOp.bypass, AluOp.add),
                                 sV)
                r.wait("scalar", sV, res_V[fc])
                if fc >= 2:
                    r.wait("scalar", sT, sqmm_T[fc - 2])
                sq_S[fc] = r.op("scalar", lambda s, d=sq[:, (fc % 2) * 512:(fc % 2 + 1) * 512],
                                so=resid[:, fc * 512:(fc + 1) * 512]:
                                s.activation(d, so, AF.Square), sS)
                if fc == 0:
                    r.wait("tensor", sV, attn_oT_V)
                    r.wait("tensor", sS, oun_S[H - 1])
                r.wait("tensor", sV, res_V[fc])
                fn = (lambda t, o=ot_a[0:1, :], l=onesc_s[:],
                      rr=resid[:, fc * 512:(fc + 1) * 512],
                      st=(fc == 0), sp=(fc == EC - 1):
                      t.matmul(o, l, rr, start=st, stop=sp, skip_group_check=True))
                r.op("tensor", fn)
                r.wait("tensor", sS, sq_S[fc])
                fn = (lambda t, o=ot_b[0:1, :], l=onesc_s[:],
                      rr=sq[:, (fc % 2) * 512:(fc % 2 + 1) * 512],
                      st=(fc == 0), sp=(fc == EC - 1):
                      t.matmul(o, l, rr, start=st, stop=sp, skip_group_check=True))
                sqmm_T[fc] = r.op("tensor", fn, sT)
            prev_res_V = res_V[EC - 1]
            stats_T = sT.n
            r.wait("vector", sT, stats_T)
            r.op("vector", lambda v: v.tensor_scalar_mul(mean_s[:], ot_a[0:1, :], 1.0 / E))
            r.op("vector", lambda v: v.tensor_scalar_mul(ex2_s[:], ot_b[0:1, :], 1.0 / E))
            r.op("vector", lambda v: v.tensor_mul(m2_s[:], mean_s[:], mean_s[:]))
            stat_V = r.op("vector", lambda v: v.tensor_sub(var_s[:], ex2_s[:], m2_s[:]), sV)
            r.wait("scalar", sV, stat_V)
            r.op("scalar", lambda s: s.activation(lnv_s[:], var_s[:], AF.Ln, bias=eps_s[0:1, 0:1]))
            rstd_S = r.op("scalar", lambda s: s.activation(rstd_s[:], lnv_s[:], AF.Exp,
                                                           scale=-0.5), sS)
            r.wait("tensor", sS, rstd_S)
            r.op("tensor", lambda t: t.matmul(mm_a[0:128, :], ones_s[0:1, 0:128], mean_s[:],
                                              start=True, stop=True, skip_group_check=True))
            bc2_T = r.op("tensor", lambda t: t.matmul(mm_b[0:128, :], ones_s[0:1, 0:128],
                                                      rstd_s[:], start=True, stop=True,
                                                      skip_group_check=True), sT)
            r.wait("vector", sT, bc2_T)
            for fc in range(EC):
                if b >= 1:
                    yq.wait(r, "vector", spill_sY[fc])
                r.op("vector", lambda v, d=tmpc[:], a=resid[:, fc * 512:(fc + 1) * 512]:
                     v.tensor_sub(d, a, mm_a[0:128, :]))
                r.op("vector", lambda v, d=x1t[:, fc * 512:(fc + 1) * 512], a=tmpc[:]:
                     v.tensor_mul(d, a, mm_b[0:128, :]))
                col = b * EC + fc
                r.op("vector", lambda v, d=tmax[:, col:col + 1],
                     a=x1t[:, fc * 512:(fc + 1) * 512]:
                     v.tensor_reduce(d, a, AX, AluOp.max, apply_absolute_value=True), sV)
                r.wait("sync", sV, sV.n)
                dsp = x1_d[fc * 128:(fc + 1) * 128, b * WIN:(b + 1) * WIN]
                ssp = x1t[:, fc * 512:(fc + 1) * 512]
                spill_sY[fc] = yq.dma(r, lambda sy, d=dsp, s=ssp:
                                      sy.dma_start(out=d, in_=s))
            prev_x1_V = sV.n

        r.op("vector", lambda v: v.tensor_reduce(amax[:, 0:1], tmax[:], AX,
                                                 AluOp.max), sV)
        r.wait("sync", sV, sV.n)
        yq.dma(r, lambda sy: sy.dma_start(out=lmax_d, in_=amax[:]))
        yq.wait_all(r, "sync")
        yq.wait_all(r, "gpsimd")

        block = e_(nc.Block())
        r.play(block)
    return nc


M32 = float(np.float32(1.5 * 2 ** 23))
M52 = 6755399441055744.0


def _emit_ind_vecs(r, nc, e_, ctx, sV, sT, sS, ones_s, idf_s, names=""):
    c0 = e_(nc.sbuf_tensor(f"indc0{names}", [128, 64], F32))
    t0 = e_(nc.sbuf_tensor(f"indt0{names}", [128, 64], F32))
    ivec = e_(nc.sbuf_tensor(f"ivec{names}", [128, 64], F32))
    avec = e_(nc.sbuf_tensor(f"avec{names}", [128, 64], F32))
    m32v = e_(nc.sbuf_tensor(f"m32v{names}", [128, 64], F32))
    m52v = e_(nc.sbuf_tensor(f"m52v{names}", [128, 64], F32))
    r.op("vector", lambda v: v.memset(c0[:], 0.25))
    r.op("vector", lambda v: v.memset(m32v[:], M32))
    r.op("vector", lambda v: v.memset(m52v[:], M52))
    r.op("vector", lambda v: v.tensor_scalar(t0[:], c0[:], M32, M32,
                                             AluOp.add, AluOp.subtract))
    r.op("vector", lambda v: v.tensor_scalar(avec[:], t0[:], 0.25, -4.0,
                                             AluOp.subtract, AluOp.mult))
    r.op("vector", lambda v: v.tensor_scalar(ivec[:], avec[:], -1.0, 1.0,
                                             AluOp.mult, AluOp.add), sV)
    return dict(ivec=ivec, avec=avec, m32v=m32v, m52v=m52v)


def _emit_quant(r, src, dst, inv_vec, qt, iv, sS, sV, w):
    q0, q1, q2, q3 = (qt[k] for k in ("q0", "q1", "q2", "q3"))
    r.wait("vector", sS, sS.n)
    r.op("vector", lambda v: v.tensor_scalar(q0[:, 0:w], src, inv_vec, 127.4,
                                             AluOp.mult, AluOp.min))
    t0v = r.op("vector", lambda v: v.tensor_scalar(q1[:, 0:w], q0[:, 0:w], -128.4,
                                                   None, AluOp.max), sV)
    r.wait("scalar", sV, t0v)
    r.op("scalar", lambda s: s.add(q2[:, 0:w], q1[:, 0:w], iv["m32v"][:, 0:1]))
    fg = r.op("scalar", lambda s: s.add(q3[:, 0:w], q1[:, 0:w], iv["m52v"][:, 0:1]), sS)
    r.wait("vector", sS, fg)
    r.op("vector", lambda v: v.tensor_scalar(q2[:, 0:w], q2[:, 0:w], M32,
                                             iv["avec"][:, 0:1], AluOp.subtract, AluOp.mult))
    r.op("vector", lambda v: v.tensor_scalar(q3[:, 0:w], q3[:, 0:w], M52,
                                             iv["ivec"][:, 0:1], AluOp.subtract, AluOp.mult))
    return r.op("vector", lambda v: v.tensor_tensor(dst, q2[:, 0:w], q3[:, 0:w],
                                                    AluOp.add), sV)



def build_l2():
    nc = bass.Bass("TRN2", target_bir_lowering=False)
    x1_d = nc.dram_tensor("x1", [E, T], F32, kind="ExternalInput").ap()
    mx_d = nc.dram_tensor("maxes", [1, 8 * 128], F32, kind="ExternalInput").ap()
    clip_d = nc.dram_tensor("clip_x", [1, 1], F32, kind="ExternalInput").ap()
    wg_d = nc.dram_tensor("w_gate", [FFI, E], F32, kind="ExternalInput").ap()
    wu_d = nc.dram_tensor("w_up", [FFI, E], F32, kind="ExternalInput").ap()
    h2_d = nc.dram_tensor("h2", [FFI, T], F32, kind="ExternalOutput").ap()
    lmax_d = nc.dram_tensor("lmax", [128, 1], F32, kind="ExternalOutput").ap()

    idb_d = nc.inline_tensor(np.eye(128, dtype=ml_dtypes.bfloat16), "c_idb").ap()
    idf_d = nc.inline_tensor(np.eye(128, dtype=np.float32), "c_idf").ap()
    ones_d = nc.inline_tensor(np.ones((1, 128), dtype=np.float32), "c_ones").ap()

    with ExitStack() as ctx:
        e_ = ctx.enter_context
        wqgT = e_(nc.sbuf_tensor("wqgT", [128, IC * 8 * 128], BF16))
        wqfT = e_(nc.sbuf_tensor("wqfT", [128, IC * 8 * 128], BF16))
        idb_s = e_(nc.sbuf_tensor("idb_s", [128, 128], BF16))
        idf_s = e_(nc.sbuf_tensor("idf_s", [128, 128], F32))
        ones_s = e_(nc.sbuf_tensor("ones_s", [1, 128], F32))
        x1s = e_(nc.sbuf_tensor("x1s", [128, EC * WIN], F32))
        xq = e_(nc.sbuf_tensor("xq", [128, EC * WIN], BF16))
        wstg = [e_(nc.sbuf_tensor(f"wstg{i}", [128, E], F32)) for i in range(2)]
        wqint = e_(nc.sbuf_tensor("wqint", [128, E], BF16))
        qt = {k: e_(nc.sbuf_tensor(f"qt_{k}", [128, E], F32)) for k in
              ("q0", "q1", "q2", "q3")}
        h2s = e_(nc.sbuf_tensor("h2s", [128, IC * WIN], F32))
        sil = [e_(nc.sbuf_tensor(f"sil{i}", [128, WIN], F32)) for i in range(2)]
        upt = [e_(nc.sbuf_tensor(f"upt{i}", [128, WIN], F32)) for i in range(2)]
        sg_all = e_(nc.sbuf_tensor("sg_all", [128, IC], F32))
        su_all = e_(nc.sbuf_tensor("su_all", [128, IC], F32))
        cg_all = e_(nc.sbuf_tensor("cg_all", [128, IC], F32))
        cu_all = e_(nc.sbuf_tensor("cu_all", [128, IC], F32))
        mx_s = e_(nc.sbuf_tensor("mx_s", [1, 8 * 128], F32))
        clip_s = e_(nc.sbuf_tensor("clip_s", [1, 64], F32))
        sig_s = e_(nc.sbuf_tensor("sig_s", [1, 64], F32))
        gmax_s = e_(nc.sbuf_tensor("gmax_s", [1, 64], F32))
        sx_s = e_(nc.sbuf_tensor("sx_s", [1, 64], F32))
        inv_s = e_(nc.sbuf_tensor("inv_s", [1, 64], F32))
        sxv = e_(nc.sbuf_tensor("sxv", [128, 1], F32))
        invv = e_(nc.sbuf_tensor("invv", [128, 1], F32))
        rm_s = e_(nc.sbuf_tensor("rm_s", [128, 2], F32))
        scw_s = e_(nc.sbuf_tensor("scw_s", [128, 2], F32))
        invw_s = e_(nc.sbuf_tensor("invw_s", [128, 2], F32))
        invw2_s = e_(nc.sbuf_tensor("invw2_s", [128, 2], F32))
        amax = e_(nc.sbuf_tensor("amax", [128, 1], F32))
        tmax = e_(nc.sbuf_tensor("tmax", [128, IC * 16], F32))
        gm2_s = e_(nc.sbuf_tensor("gm2_s", [1, 2], F32))
        zz_s = e_(nc.sbuf_tensor("zz_s", [128, 2], F32))
        tr_a = e_(nc.psum_tensor("tr_a", [128, 1024], BF16))
        tr_b = e_(nc.psum_tensor("tr_b", [128, 1024], BF16))
        mm_a = e_(nc.psum_tensor("mm_a", [128, 512], F32))
        mm_b = e_(nc.psum_tensor("mm_b", [128, 512], F32))
        sc_a = e_(nc.psum_tensor("sc_a", [128, 512], F32))
        tr_pp, mm_pp = [tr_a, tr_b], [mm_a, mm_b]

        gq = DmaQ(ctx, nc, "gq", "gpsimd")
        yq = DmaQ(ctx, nc, "yq", "sync")
        sT = Ctr(ctx, nc, "sT")
        sS = Ctr(ctx, nc, "sS")
        sV = Ctr(ctx, nc, "sV")
        r = Rec()

        for dst, srcc in ((idb_s[:], idb_d), (idf_s[:], idf_d), (ones_s[:], ones_d),
                          (mx_s[:], mx_d), (clip_s[0:1, 0:1], clip_d)):
            gq.dma(r, lambda g, d=dst, s=srcc: g.dma_start(out=d, in_=s))
        gq.wait_all(r, "vector")
        gq.wait_all(r, "tensor")
        gq.wait_all(r, "scalar")
        iv = _emit_ind_vecs(r, nc, e_, ctx, sV, sT, sS, ones_s, idf_s)
        r.op("vector", lambda v: v.memset(zz_s[:], 0.0))
        for pad in (clip_s[0:1, 1:64], sig_s[:], gmax_s[0:1, 1:64], sx_s[:],
                    inv_s[:], rm_s[:], scw_s[:], invw_s[:], invw2_s[:]):
            r.op("vector", lambda v, a=pad: v.memset(a, 1.0))
        r.op("vector", lambda v: v.engine_nop(), sV)
        gm1 = r.op("vector", lambda v: v.tensor_reduce(gmax_s[0:1, 0:1], mx_s[:], AX,
                                                       AluOp.max), sV)
        r.wait("scalar", sV, gm1)
        sg1 = r.op("scalar", lambda s: s.activation(sig_s[:], clip_s[:], AF.Sigmoid), sS)
        r.wait("vector", sS, sg1)
        r.op("vector", lambda v: v.tensor_mul(sx_s[:], sig_s[:], gmax_s[:]))
        r.op("vector", lambda v: v.tensor_scalar(
            sx_s[:], sx_s[:], 1.0 / 127.0, 0.0, AluOp.mult, AluOp.add))
        sxd = r.op("vector", lambda v: v.reciprocal(inv_s[:], sx_s[:]), sV)
        r.wait("tensor", sV, sxd)
        r.op("tensor", lambda t: t.matmul(sc_a[0:128, 0:1], ones_s[0:1, 0:128],
                                          sx_s[0:1, 0:1], start=True, stop=True,
                                          skip_group_check=True))
        r.op("tensor", lambda t: t.matmul(sc_a[0:128, 1:2], ones_s[0:1, 0:128],
                                          inv_s[0:1, 0:1], start=True, stop=True,
                                          skip_group_check=True), sT)
        bc1 = sT.n
        r.wait("scalar", sT, bc1)
        r.op("scalar", lambda s: s.copy(sxv[:], sc_a[0:128, 0:1]))
        bc2 = r.op("scalar", lambda s: s.copy(invv[:], sc_a[0:128, 1:2]), sS)
        r.wait("vector", sS, bc2)
        setup_V, setup_S = sV.n, sS.n

        last_stg_T = {}
        last_tr_S = {}
        gidx = 0
        r.wait("vector", sV, setup_V)
        for wt_d, wqT, s_all, cc_all in ((wg_d, wqgT, sg_all, cg_all),
                                         (wu_d, wqfT, su_all, cu_all)):
            for ic in range(IC):
                par = gidx % 2
                stg = wstg[par]
                if par in last_stg_T:
                    r.wait("gpsimd", sT, last_stg_T[par])
                tok_w = gq.dma(r, lambda g, d=stg[:], s=wt_d[ic * 128:(ic + 1) * 128, :]:
                               g.dma_start(out=d, in_=s))
                gq.wait(r, "vector", tok_w)
                if gidx >= 1:
                    r.wait("vector", sS, last_sc_S)
                rd = r.op("vector", lambda v, a=stg[:]: v.tensor_reduce(
                    rm_s[:, 0:1], a, AX, AluOp.max, apply_absolute_value=True), sV)
                r.wait("scalar", sV, rd)
                r.op("scalar", lambda s: s.mul(scw_s[:, 0:1], rm_s[:, 0:1], 1.0 / 127.0))
                r.op("scalar", lambda s, d=s_all[:, ic:ic + 1]: s.copy(d, scw_s[:, 0:1]))
                scd = r.op("scalar", lambda s, d=cc_all[:, ic:ic + 1]:
                           s.mul(d, scw_s[:, 0:1], sxv[:, 0:1]), sS)
                r.wait("vector", sS, scd)
                rc = r.op("vector", lambda v: v.reciprocal(invw_s[:, 0:1], scw_s[:, 0:1]), sV)
                r.wait("scalar", sV, rc)
                r.op("scalar", lambda s: s.copy(invw2_s[:, 0:1], invw_s[:, 0:1]), sS)
                last_sc_S = sS.n
                if gidx >= 1:
                    r.wait("vector", sT, last_wq_T)
                qv = _emit_quant(r, stg[:], wqint[:], invw2_s[:, 0:1], qt, iv, sS, sV, E)
                r.wait("tensor", sV, qv)
                if par in last_tr_S:
                    r.wait("tensor", sS, last_tr_S[par])
                bank = tr_pp[par]
                for hc in range(8):
                    fn = (lambda t, o=bank[:, hc * 128:(hc + 1) * 128],
                          i_=wqint[:, hc * 128:(hc + 1) * 128], idn=idb_s[:]:
                          t.transpose(o, i_, idn))
                    r.op("tensor", fn, sT if hc == 7 else None)
                last_stg_T[par] = sT.n
                last_wq_T = sT.n
                r.wait("scalar", sT, sT.n)
                r.op("scalar", lambda s, d=wqT[:, ic * 1024:(ic + 1) * 1024],
                     so=bank[:]: s.copy(d, so), sS)
                last_tr_S[par] = sS.n
                gidx += 1
        cgv = sV.n

        spill = {}
        x1_used_V = 0
        evac_S = {}
        h2_V = {}
        for tt in range(T // WIN):
            if tt > 0:
                r.wait("gpsimd", sV, x1_used_V)
            srcc = x1_d[:, tt * WIN:(tt + 1) * WIN].rearrange("(c p) t -> p c t", p=128)
            dstc = x1s[:].rearrange("p (c t) -> p c t", t=WIN)
            tok_x = gq.dma(r, lambda g, d=dstc, s=srcc: g.dma_start(out=d, in_=s))
            gq.wait(r, "vector", tok_x)
            r.wait("vector", sS, sS.n)
            if tt == 0:
                r.wait("vector", sV, cgv)
            qvs = []
            for fc in range(EC):
                if tt > 0:
                    r.wait("vector", sT, mm_done_T)
                qv = _emit_quant(r, x1s[:, fc * WIN:(fc + 1) * WIN],
                                 xq[:, fc * WIN:(fc + 1) * WIN], invv[:], qt, iv,
                                 sS, sV, WIN)
                qvs.append(qv)
            x1_used_V = sV.n
            r.wait("tensor", sV, qvs[-1])
            if tt == 0:
                r.wait("tensor", sS, sS.n)
            for ic in range(IC):
                for which in range(2):
                    bank = mm_pp[which]
                    wqT = wqgT if which == 0 else wqfT
                    if ic > 0:
                        r.wait("tensor", sS, evac_S[(ic - 1, which)])
                    elif tt > 0:
                        r.wait("tensor", sS, evac_S[(IC - 1, which)])
                    for hc in range(EC):
                        fn = (lambda t, o=bank[:],
                              l=wqT[:, ic * 1024 + hc * 128: ic * 1024 + hc * 128 + 128],
                              rr=xq[:, hc * WIN:(hc + 1) * WIN],
                              st=(hc == 0), sp=(hc == EC - 1):
                              t.matmul(o, l, rr, start=st, stop=sp,
                                       skip_group_check=True))
                        r.op("tensor", fn, sT if hc == EC - 1 else None)
                    mmt = sT.n
                    r.wait("scalar", sT, mmt)
                    if which == 0:
                        if ic >= 2:
                            r.wait("scalar", sV, h2_V[ic - 2])
                        elif tt > 0:
                            r.wait("scalar", sV, h2_V[IC - 2 + ic])
                        evac_S[(ic, 0)] = r.op(
                            "scalar", lambda s, d=sil[ic % 2][:], so=bank[:],
                            sc=cg_all[:, ic:ic + 1]:
                            s.activation(d, so, AF.Silu, scale=sc), sS)
                    else:
                        evac_S[(ic, 1)] = r.op(
                            "scalar", lambda s, d=upt[ic % 2][:], so=bank[:],
                            m=cu_all[:, ic:ic + 1]: s.mul(d, so, m), sS)
                if True:
                    r.wait("vector", sS, evac_S[(ic, 1)])
                    if tt > 0 and ic < IC:
                        yq.wait(r, "vector", spill.get((tt - 1, ic // 11)))
                    r.op("vector", lambda v, d=h2s[:, ic * WIN:(ic + 1) * WIN],
                         a=sil[ic % 2][:], bb=upt[ic % 2][:]:
                         v.tensor_mul(d, a, bb))
                    col = tt * IC + ic
                    h2_V[ic] = r.op("vector", lambda v, d=tmax[:, col:col + 1],
                                    a=h2s[:, ic * WIN:(ic + 1) * WIN]:
                                    v.tensor_reduce(d, a, AX, AluOp.max,
                                                    apply_absolute_value=True), sV)
            mm_done_T = sT.n
            for half in range(2):
                lo = half * 11
                r.wait("sync", sV, h2_V[lo + 10])
                dsp = h2_d[:, tt * WIN:(tt + 1) * WIN].rearrange(
                    "(c p) t -> p c t", p=128)[:, lo:lo + 11, :]
                ssp = h2s[:].rearrange("p (c t) -> p c t", t=WIN)[:, lo:lo + 11, :]
                spill[(tt, half)] = yq.dma(r, lambda sy, d=dsp, s=ssp:
                                           sy.dma_start(out=d, in_=s))

        r.op("vector", lambda v: v.tensor_reduce(amax[:, 0:1], tmax[:], AX,
                                                 AluOp.max), sV)
        r.wait("sync", sV, sV.n)
        yq.dma(r, lambda sy: sy.dma_start(out=lmax_d, in_=amax[:]))
        yq.wait_all(r, "sync")
        yq.wait_all(r, "gpsimd")

        block = e_(nc.Block())
        r.play(block)
    return nc


L3_DEBUG = False


def build_l3():
    nc = bass.Bass("TRN2", target_bir_lowering=False)
    h2_d = nc.dram_tensor("h2", [FFI, T], F32, kind="ExternalInput").ap()
    mx_d = nc.dram_tensor("maxes", [1, 8 * 128], F32, kind="ExternalInput").ap()
    clip_d = nc.dram_tensor("clip_down", [1, 1], F32, kind="ExternalInput").ap()
    x1_d = nc.dram_tensor("x1", [E, T], F32, kind="ExternalInput").ap()
    wd_d = nc.dram_tensor("w_down", [E, FFI], F32, kind="ExternalInput").ap()
    out_d = nc.dram_tensor("out", [T, E], F32, kind="ExternalOutput").ap()
    if L3_DEBUG:
        dbg_resid_d = nc.dram_tensor("dbg_resid", [128, EC * WIN], BF16,
                                     kind="ExternalOutput").ap()
        dbg_hq_d = nc.dram_tensor("dbg_hq", [128, IC * WIN], BF16,
                                  kind="ExternalOutput").ap()
        dbg_sc_d = nc.dram_tensor("dbg_sc", [128, 16], F32, kind="ExternalOutput").ap()

    idb_d = nc.inline_tensor(np.eye(128, dtype=ml_dtypes.bfloat16), "c_idb").ap()
    idf_d = nc.inline_tensor(np.eye(128, dtype=np.float32), "c_idf").ap()
    ones_d = nc.inline_tensor(np.ones((1, 128), dtype=np.float32), "c_ones").ap()
    onesc_d = nc.inline_tensor(np.ones((128, 1), dtype=ml_dtypes.bfloat16), "c_onescol").ap()

    with ExitStack() as ctx:
        e_ = ctx.enter_context
        wqdT = e_(nc.sbuf_tensor("wqdT", [128, IC * 8 * 128], BF16))
        idb_s = e_(nc.sbuf_tensor("idb_s", [128, 128], BF16))
        idf_s = e_(nc.sbuf_tensor("idf_s", [128, 128], F32))
        ones_s = e_(nc.sbuf_tensor("ones_s", [1, 128], F32))
        onesc_s = e_(nc.sbuf_tensor("onesc_s", [128, 1], BF16))
        h2s = e_(nc.sbuf_tensor("h2s", [128, 11 * WIN], F32))
        hq = e_(nc.sbuf_tensor("hq", [128, IC * WIN], BF16))
        x1s = e_(nc.sbuf_tensor("x1s", [128, EC * WIN], F32))
        resid = e_(nc.sbuf_tensor("resid", [128, EC * WIN], BF16))
        sq = e_(nc.sbuf_tensor("sq", [128, 2 * WIN], BF16))
        outT = e_(nc.sbuf_tensor("outT", [128, EC * WIN], BF16))
        outstg = e_(nc.sbuf_tensor("outstg", [128, 2 * E], F32))
        wstg = [h2s[:, 0:FFI], h2s[:, FFI:2 * FFI]]
        wqint = e_(nc.sbuf_tensor("wqint", [128, FFI], BF16))
        qt = {k: e_(nc.sbuf_tensor(f"qt_{k}", [128, E], F32)) for k in
              ("q0", "q1", "q2", "q3")}
        sd_all = e_(nc.sbuf_tensor("sd_all", [128, 8], F32))
        cd_all = e_(nc.sbuf_tensor("cd_all", [128, 8], F32))
        mx_s = e_(nc.sbuf_tensor("mx_s", [1, 8 * 128], F32))
        clip_s = e_(nc.sbuf_tensor("clip_s", [1, 64], F32))
        sig_s = e_(nc.sbuf_tensor("sig_s", [1, 64], F32))
        gmax_s = e_(nc.sbuf_tensor("gmax_s", [1, 64], F32))
        s2_s = e_(nc.sbuf_tensor("s2_s", [1, 64], F32))
        inv_s = e_(nc.sbuf_tensor("inv_s", [1, 64], F32))
        s2v = e_(nc.sbuf_tensor("s2v", [128, 1], F32))
        invv = e_(nc.sbuf_tensor("invv", [128, 1], F32))
        rm_s = e_(nc.sbuf_tensor("rm_s", [128, 2], F32))
        scw_s = e_(nc.sbuf_tensor("scw_s", [128, 2], F32))
        invw_s = e_(nc.sbuf_tensor("invw_s", [128, 2], F32))
        invw2_s = e_(nc.sbuf_tensor("invw2_s", [128, 2], F32))
        mean_s = e_(nc.sbuf_tensor("mean_s", [1, WIN], F32))
        ex2_s = e_(nc.sbuf_tensor("ex2_s", [1, WIN], F32))
        m2_s = e_(nc.sbuf_tensor("m2_s", [1, WIN], F32))
        var_s = e_(nc.sbuf_tensor("var_s", [1, WIN], F32))
        lnv_s = e_(nc.sbuf_tensor("lnv_s", [1, WIN], F32))
        rstd_s = e_(nc.sbuf_tensor("rstd_s", [1, WIN], F32))
        tmpc = e_(nc.sbuf_tensor("tmpc", [128, WIN], BF16))
        eps_s = e_(nc.sbuf_tensor("eps_s", [1, 2], F32))
        zz_s = e_(nc.sbuf_tensor("zz_s", [128, 2], F32))
        tr_a = e_(nc.psum_tensor("tr_a", [128, 1024], BF16))
        tr_b = e_(nc.psum_tensor("tr_b", [128, 1024], BF16))
        mm_a = e_(nc.psum_tensor("mm_a", [128, 512], F32))
        mm_b = e_(nc.psum_tensor("mm_b", [128, 512], F32))
        sc_a = e_(nc.psum_tensor("sc_a", [128, 512], F32))
        ot_a = e_(nc.psum_tensor("ot_a", [128, 512], F32))
        ot_b = e_(nc.psum_tensor("ot_b", [128, 512], F32))
        tr_pp, mm_pp = [tr_a, tr_b], [mm_a, mm_b]

        gq = DmaQ(ctx, nc, "gq", "gpsimd")
        yq = DmaQ(ctx, nc, "yq", "sync")
        sT = Ctr(ctx, nc, "sT")
        sS = Ctr(ctx, nc, "sS")
        sV = Ctr(ctx, nc, "sV")
        r = Rec()

        for dst, srcc in ((idb_s[:], idb_d), (idf_s[:], idf_d), (ones_s[:], ones_d),
                          (onesc_s[:], onesc_d), (mx_s[:], mx_d), (clip_s[0:1, 0:1], clip_d)):
            gq.dma(r, lambda g, d=dst, s=srcc: g.dma_start(out=d, in_=s))
        gq.wait_all(r, "vector")
        gq.wait_all(r, "tensor")
        gq.wait_all(r, "scalar")
        iv = _emit_ind_vecs(r, nc, e_, ctx, sV, sT, sS, ones_s, idf_s)
        r.op("vector", lambda v: v.memset(eps_s[:], 1e-5))
        r.op("vector", lambda v: v.memset(zz_s[:], 0.0))
        for pad in (clip_s[0:1, 1:64], sig_s[:], gmax_s[0:1, 1:64], s2_s[:],
                    inv_s[:], rm_s[:], scw_s[:], invw_s[:], invw2_s[:]):
            r.op("vector", lambda v, a=pad: v.memset(a, 1.0))
        r.op("vector", lambda v: v.engine_nop(), sV)
        gm1 = r.op("vector", lambda v: v.tensor_reduce(gmax_s[0:1, 0:1], mx_s[:], AX,
                                                        AluOp.max), sV)
        r.wait("scalar", sV, gm1)
        sg1 = r.op("scalar", lambda s: s.activation(sig_s[:], clip_s[:], AF.Sigmoid), sS)
        r.wait("vector", sS, sg1)
        r.op("vector", lambda v: v.tensor_mul(s2_s[:], sig_s[:], gmax_s[:]))
        r.op("vector", lambda v: v.tensor_scalar(
            s2_s[:], s2_s[:], 1.0 / 127.0, 0.0, AluOp.mult, AluOp.add))
        sxd = r.op("vector", lambda v: v.reciprocal(inv_s[:], s2_s[:]), sV)
        r.wait("tensor", sV, sxd)
        r.op("tensor", lambda t: t.matmul(sc_a[0:128, 0:1], ones_s[0:1, 0:128],
                                          s2_s[0:1, 0:1], start=True, stop=True,
                                          skip_group_check=True))
        r.op("tensor", lambda t: t.matmul(sc_a[0:128, 1:2], ones_s[0:1, 0:128],
                                          inv_s[0:1, 0:1], start=True, stop=True,
                                          skip_group_check=True), sT)
        bc1 = sT.n
        r.wait("scalar", sT, bc1)
        r.op("scalar", lambda s: s.copy(s2v[:], sc_a[0:128, 0:1]))
        bc2 = r.op("scalar", lambda s: s.copy(invv[:], sc_a[0:128, 1:2]), sS)
        r.wait("vector", sS, bc2)
        setup_V = sV.n

        last_stg_T = {}
        last_tr_S = {}
        last_wq_T = 0
        evac8_S = {}
        r.wait("vector", sV, setup_V)
        for hc in range(8):
            par = hc % 2
            stg = wstg[par]
            if par in last_stg_T:
                r.wait("gpsimd", sT, last_stg_T[par])
            tok_w = gq.dma(r, lambda g, d=stg, s=wd_d[hc * 128:(hc + 1) * 128, :]:
                           g.dma_start(out=d, in_=s))
            gq.wait(r, "vector", tok_w)
            if hc >= 1:
                r.wait("vector", sS, last_sc_S)
            rd = r.op("vector", lambda v, a=stg: v.tensor_reduce(
                rm_s[:, 0:1], a, AX, AluOp.max, apply_absolute_value=True), sV)
            r.wait("scalar", sV, rd)
            r.op("scalar", lambda s: s.mul(scw_s[:, 0:1], rm_s[:, 0:1], 1.0 / 127.0))
            r.op("scalar", lambda s, d=sd_all[:, hc:hc + 1]: s.copy(d, scw_s[:, 0:1]))
            scd = r.op("scalar", lambda s, d=cd_all[:, hc:hc + 1]:
                       s.mul(d, scw_s[:, 0:1], s2v[:, 0:1]), sS)
            r.wait("vector", sS, scd)
            rc = r.op("vector", lambda v: v.reciprocal(invw_s[:, 0:1], scw_s[:, 0:1]), sV)
            r.wait("scalar", sV, rc)
            r.op("scalar", lambda s: s.copy(invw2_s[:, 0:1], invw_s[:, 0:1]), sS)
            last_sc_S = sS.n
            if hc >= 1:
                r.wait("vector", sT, last_wq_T)
            for sl in range(0, FFI, E):
                wsl = min(E, FFI - sl)
                _emit_quant(r, stg[:, sl:sl + wsl], wqint[:, sl:sl + wsl], invw2_s[:, 0:1],
                            qt, iv, sS, sV, wsl)
            qv = sV.n
            r.wait("tensor", sV, qv)
            if par in last_tr_S:
                r.wait("tensor", sS, last_tr_S[par])
            done = 0
            while done < IC:
                nn = min(8, IC - done)
                bank = tr_pp[(done // 8) % 2]
                if done >= 16:
                    r.wait("tensor", sS, evac8_S[done % 16])
                for j in range(nn):
                    fn = (lambda t, o=bank[:, j * 128:(j + 1) * 128],
                          i_=wqint[:, (done + j) * 128:(done + j + 1) * 128],
                          idn=idb_s[:]: t.transpose(o, i_, idn))
                    r.op("tensor", fn, sT if j == nn - 1 else None)
                r.wait("scalar", sT, sT.n)
                dstw = wqdT[:].rearrange("p (ic g) -> p ic g", g=8 * 128)[
                    :, done:done + nn, hc * 128:(hc + 1) * 128]
                srcw = bank[:, 0:nn * 128].rearrange("p (j g) -> p j g", g=128)
                evac8_S[done % 16] = r.op("scalar", lambda s, d=dstw, so=srcw:
                                          s.copy(d, so), sS)
                done += nn
            last_stg_T[par] = sT.n
            last_wq_T = sT.n
            last_tr_S[par] = sS.n
        cdv = sV.n
        w_evac_done_S = sS.n
        w_done_T = sT.n

        h2_used_V = 0
        x1_used_V = 0
        out_spill = {}
        trev_S = {0: w_evac_done_S, 1: w_evac_done_S}
        for tt in range(T // WIN):
            srch = h2_d[:, tt * WIN:(tt + 1) * WIN].rearrange("(c p) t -> p c t", p=128)
            dsth = h2s[:].rearrange("p (c t) -> p c t", t=WIN)
            if tt > 0:
                r.wait("gpsimd", sV, x1_used_V)
            srcx = x1_d[:, tt * WIN:(tt + 1) * WIN].rearrange("(c p) t -> p c t", p=128)
            dstx = x1s[:].rearrange("p (c t) -> p c t", t=WIN)
            tok_x = gq.dma(r, lambda g, d=dstx, s=srcx: g.dma_start(out=d, in_=s))
            if tt == 0:
                r.wait("vector", sV, cdv)
            for half in range(2):
                lo = half * 11
                nn = 11
                if tt == 0 and half == 0:
                    r.wait("gpsimd", sT, w_done_T)
                r.wait("gpsimd", sV, h2_used_V)
                tok_h = gq.dma(r, lambda g, d=dsth[:, 0:nn, :],
                               s=srch[:, lo:lo + nn, :]: g.dma_start(out=d, in_=s))
                gq.wait(r, "vector", tok_h)
                for icl in range(nn):
                    ic = lo + icl
                    if tt > 0:
                        r.wait("vector", sT, mm_done_T)
                    _emit_quant(r, h2s[:, icl * WIN:(icl + 1) * WIN],
                                hq[:, ic * WIN:(ic + 1) * WIN], invv[:], qt, iv,
                                sS, sV, WIN)
                h2_used_V = sV.n
                r.wait("tensor", sV, h2_used_V)
            if tt == 0:
                r.wait("tensor", sS, sS.n)
            else:
                r.wait("tensor", sV, prev_norm_V)
            res_V, sq_S, sqmm_T = {}, {}, {}
            for hc in range(EC):
                bank = mm_pp[hc % 2]
                if hc >= 2:
                    r.wait("tensor", sV, res_V[hc - 2])
                elif tt > 0:
                    r.wait("tensor", sV, prev_res_last)
                for ic in range(IC):
                    fn = (lambda t, o=bank[:],
                          l=wqdT[:, (ic * 8 + hc) * 128:(ic * 8 + hc) * 128 + 128],
                          rr=hq[:, ic * WIN:(ic + 1) * WIN],
                          st=(ic == 0), sp=(ic == IC - 1):
                          t.matmul(o, l, rr, start=st, stop=sp, skip_group_check=True))
                    r.op("tensor", fn, sT if ic == IC - 1 else None)
                r.wait("vector", sT, sT.n)
                gq.wait(r, "vector", tok_x)
                res_V[hc] = r.op("vector", lambda v, d=resid[:, hc * 512:(hc + 1) * 512],
                                 a=bank[:], sc=cd_all[:, hc:hc + 1],
                                 x_=x1s[:, hc * 512:(hc + 1) * 512]:
                                 v.scalar_tensor_tensor(d, a, sc, x_, AluOp.mult,
                                                        AluOp.add), sV)
                r.wait("scalar", sV, res_V[hc])
                if hc >= 2:
                    r.wait("scalar", sT, sqmm_T[hc - 2])
                elif tt > 0:
                    r.wait("scalar", sT, prev_sqmm_last)
                sq_S[hc] = r.op("scalar", lambda s, d=sq[:, (hc % 2) * 512:(hc % 2 + 1) * 512],
                                so=resid[:, hc * 512:(hc + 1) * 512]:
                                s.activation(d, so, AF.Square), sS)
                r.wait("tensor", sV, res_V[hc])
                fn = (lambda t, o=ot_a[0:1, :], l=onesc_s[:],
                      rr=resid[:, hc * 512:(hc + 1) * 512],
                      st=(hc == 0), sp=(hc == EC - 1):
                      t.matmul(o, l, rr, start=st, stop=sp, skip_group_check=True))
                r.op("tensor", fn)
                r.wait("tensor", sS, sq_S[hc])
                fn = (lambda t, o=ot_b[0:1, :], l=onesc_s[:],
                      rr=sq[:, (hc % 2) * 512:(hc % 2 + 1) * 512],
                      st=(hc == 0), sp=(hc == EC - 1):
                      t.matmul(o, l, rr, start=st, stop=sp, skip_group_check=True))
                sqmm_T[hc] = r.op("tensor", fn, sT)
            x1_used_V = res_V[EC - 1]
            prev_res_last = res_V[EC - 1]
            prev_sqmm_last = sqmm_T[EC - 1]
            mm_done_T = sT.n
            stats_T = sT.n
            r.wait("vector", sT, stats_T)
            r.op("vector", lambda v: v.tensor_scalar_mul(mean_s[:], ot_a[0:1, :], 1.0 / E))
            r.op("vector", lambda v: v.tensor_scalar_mul(ex2_s[:], ot_b[0:1, :], 1.0 / E))
            r.op("vector", lambda v: v.tensor_mul(m2_s[:], mean_s[:], mean_s[:]))
            stat_V = r.op("vector", lambda v: v.tensor_sub(var_s[:], ex2_s[:], m2_s[:]), sV)
            r.wait("scalar", sV, stat_V)
            r.op("scalar", lambda s: s.activation(lnv_s[:], var_s[:], AF.Ln,
                                                  bias=eps_s[0:1, 0:1]))
            rstd_S = r.op("scalar", lambda s: s.activation(rstd_s[:], lnv_s[:], AF.Exp,
                                                           scale=-0.5), sS)
            r.wait("tensor", sS, rstd_S)
            r.op("tensor", lambda t: t.matmul(mm_a[0:128, :], ones_s[0:1, 0:128],
                                              mean_s[:], start=True, stop=True,
                                              skip_group_check=True))
            bc2_T = r.op("tensor", lambda t: t.matmul(mm_b[0:128, :], ones_s[0:1, 0:128],
                                                      rstd_s[:], start=True, stop=True,
                                                      skip_group_check=True), sT)
            r.wait("vector", sT, bc2_T)
            for hc in range(EC):
                r.op("vector", lambda v, d=tmpc[:], a=resid[:, hc * 512:(hc + 1) * 512]:
                     v.tensor_sub(d, a, mm_a[0:128, :]))
                r.op("vector", lambda v, d=outT[:, hc * 512:(hc + 1) * 512], a=tmpc[:]:
                     v.tensor_mul(d, a, mm_b[0:128, :]), sV)
            prev_norm_V = sV.n
            r.wait("tensor", sV, prev_norm_V)
            for g in range(4):
                bank = tr_pp[g % 2]
                r.wait("tensor", sS, trev_S.get(g % 2, 0))
                for k in range(8):
                    tc, ecc = g, k
                    fn = (lambda t, o=bank[:, k * 128:(k + 1) * 128],
                          i_=outT[:, k * 512 + g * 128: k * 512 + g * 128 + 128],
                          idn=idb_s[:]: t.transpose(o, i_, idn))
                    r.op("tensor", fn, sT if k == 7 else None)
                r.wait("scalar", sT, sT.n)
                prevg = (tt, g - 2) if g >= 2 else (tt - 1, g + 2)
                yq.wait(r, "scalar", out_spill.get(prevg))
                r.op("scalar", lambda s, d=outstg[:, (g % 2) * E:(g % 2 + 1) * E],
                     so=bank[:]: s.copy(d, so), sS)
                trev_S[g % 2] = sS.n
                r.wait("sync", sS, sS.n)
                dsp = out_d[tt * WIN + g * 128: tt * WIN + (g + 1) * 128, :]
                ssp = outstg[:, (g % 2) * E:(g % 2 + 1) * E]
                out_spill[(tt, g)] = yq.dma(r, lambda sy, d=dsp, s=ssp:
                                            sy.dma_start(out=d, in_=s))

        if L3_DEBUG:
            dbgs = e_(nc.sbuf_tensor("dbgs", [128, 16], F32))
            r.wait("vector", sT, sT.n)
            r.op("vector", lambda v: v.tensor_copy(dbgs[:, 0:8], sd_all[:]))
            r.op("vector", lambda v: v.tensor_copy(dbgs[:, 8:16], cd_all[:]), sV)
            r.wait("sync", sV, sV.n)
            yq.dma(r, lambda sy: sy.dma_start(out=dbg_resid_d, in_=resid[:]))
            yq.dma(r, lambda sy: sy.dma_start(out=dbg_hq_d, in_=hq[:]))
            yq.dma(r, lambda sy: sy.dma_start(out=dbg_sc_d, in_=dbgs[:]))
        yq.wait_all(r, "sync")
        yq.wait_all(r, "gpsimd")
        block = e_(nc.Block())
        r.play(block)
    return nc


_NC_CACHE = {}


def kernel(L0_attn_x, L0_attn_w_in, L0_attn_w_out, L0_ffn_clip_x, L0_ffn_clip_down,
           L0_ffn_w_gate, L0_ffn_w_up, L0_ffn_w_down):
    x = np.ascontiguousarray(np.asarray(L0_attn_x).reshape(NCORES, T, E))
    w_in = np.ascontiguousarray(np.asarray(L0_attn_w_in))
    w_out = np.ascontiguousarray(np.asarray(L0_attn_w_out))
    w_gate = np.ascontiguousarray(np.asarray(L0_ffn_w_gate))
    w_up = np.ascontiguousarray(np.asarray(L0_ffn_w_up))
    w_down = np.ascontiguousarray(np.asarray(L0_ffn_w_down))
    clip_x = np.ascontiguousarray(np.asarray(L0_ffn_clip_x).reshape(1, 1))
    clip_down = np.ascontiguousarray(np.asarray(L0_ffn_clip_down).reshape(1, 1))

    if "l1" not in _NC_CACHE:
        _NC_CACHE["l1"] = build_l1()
        _NC_CACHE["l2"] = build_l2()
        _NC_CACHE["l3"] = build_l3()
    cores = list(range(NCORES))

    res1 = run_bass_kernel_spmd(
        _NC_CACHE["l1"],
        [{"x": x[c], "w_in": w_in, "w_out": w_out} for c in cores], core_ids=cores)
    x1s = [np.asarray(res1.results[c]["x1"]) for c in cores]
    maxes1 = np.ascontiguousarray(
        np.stack([np.asarray(res1.results[c]["lmax"]).reshape(128) for c in cores])
        .reshape(1, 1024))

    res2 = run_bass_kernel_spmd(
        _NC_CACHE["l2"],
        [{"x1": x1s[c], "maxes": maxes1, "clip_x": clip_x,
          "w_gate": w_gate, "w_up": w_up} for c in cores], core_ids=cores)
    h2s = [np.asarray(res2.results[c]["h2"]) for c in cores]
    maxes2 = np.ascontiguousarray(
        np.stack([np.asarray(res2.results[c]["lmax"]).reshape(128) for c in cores])
        .reshape(1, 1024))

    res3 = run_bass_kernel_spmd(
        _NC_CACHE["l3"],
        [{"h2": h2s[c], "maxes": maxes2, "clip_down": clip_down,
          "x1": x1s[c], "w_down": w_down} for c in cores], core_ids=cores)
    out = np.stack([np.asarray(res3.results[c]["out"]) for c in cores])
    return out.reshape(B, S, E)
